# version 02 (final)
# baseline (speedup 1.0000x reference)
import math
import numpy as np
import ml_dtypes
import concourse.bass as bass
import concourse.mybir as mybir
from concourse.bass_utils import run_bass_kernel_spmd
from contextlib import ExitStack

F32 = mybir.dt.float32
BF16 = mybir.dt.bfloat16
AF = mybir.ActivationFunctionType
ALU = mybir.AluOpType

D = 4096
DT = 32
C1 = 2304
C2 = 2048
NH = 32
SCALE = 1.0 / math.sqrt(128.0)
EPS = 1e-5
BF = ml_dtypes.bfloat16


def segs(C, base=0):
    out = []
    c = base
    while c < C:
        w_ = min(512 - (c % 512), C - c)
        out.append((c, w_))
        c += w_
    return out


class Ctr:
    def __init__(self, h):
        self.h = h
        self.n = 0


class Prog:
    ENGS = ("sync", "gpsimd", "tensor", "vector", "scalar")

    def __init__(self):
        self.q = {e: [] for e in self.ENGS}

    def emit(self, eng, fn, waits=(), inc=None, amt=1):
        tok = None
        if inc is not None:
            inc.n += amt
            tok = inc.n
        ws = tuple((c.h, v) for (c, v) in waits if v is not None)
        self.q[eng].append((fn, ws, inc.h if inc is not None else None, amt))
        return tok

    def run(self, eng, engine):
        for (fn, ws, ih, amt) in self.q[eng]:
            for (h, v) in ws:
                engine.wait_ge(h, v)
            if fn is None:
                continue
            ins = fn(engine)
            if ih is not None:
                ins.then_inc(ih, amt)


def build_program(debug=False):
    nc = bass.Bass("TRN2", target_bir_lowering=False)

    din = lambda n, s: nc.dram_tensor(n, s, F32, kind="ExternalInput").ap()
    xT = din("xT", [D, C1])
    wq = din("wq", [DT, 128, D])
    wo = din("wo", [DT, 128, D])
    wf = din("wf", [DT, 128, D])
    wkv = din("wkv", [2, 128, D])
    biasC = din("biasC", [128, 32])
    loraA = din("loraA", [16, 32, D])
    loraBT = din("loraBT", [16, 32, D])
    lashr = din("lashr", [32, D])
    lbshrT = din("lbshrT", [32, D])
    gateA = din("gateA", [128, 64])
    tabsD = din("tabsD", [128, 2048])
    cosR = din("cosR", [128, C1])
    sinR = din("sinR", [128, C1])
    identb = nc.dram_tensor("identb", [128, 128], BF16, kind="ExternalInput").ap()
    id32b = nc.dram_tensor("id32b", [32, 32], BF16, kind="ExternalInput").ap()
    onesf = din("onesf", [64, 128])
    onescb = nc.dram_tensor("onescb", [128, 1], BF16, kind="ExternalInput").ap()

    outT = nc.dram_tensor("outT", [D, C2], F32, kind="ExternalOutput").ap()
    dbgA = nc.dram_tensor("dbgA", [D, C1], F32, kind="ExternalOutput").ap() if debug else None
    dbgB = nc.dram_tensor("dbgB", [D, C2], F32, kind="ExternalOutput").ap() if debug else None
    dbgC = nc.dram_tensor("dbgC", [128, C1], F32, kind="ExternalOutput").ap() if debug else None
    dbgD = nc.dram_tensor("dbgD", [128, 224], F32, kind="ExternalOutput").ap() if debug else None
    dbgE = nc.dram_tensor("dbgE", [128, 6144], BF16, kind="ExternalOutput").ap() if debug else None
    y1T = nc.dram_tensor("y1T", [D, C1], F32, kind="ExternalOutput").ap()
    y2T = nc.dram_tensor("y2T", [D, C2], F32, kind="ExternalOutput").ap()
    qrS = nc.dram_tensor("qrS", [NH, 128, C2], BF16, kind="ExternalOutput").ap()
    dbgK = nc.dram_tensor("dbgK", [128, C1], BF16, kind="ExternalOutput").ap()

    P = Prog()

    with ExitStack() as ctx:
        e = ctx.enter_context
        BIG = e(nc.sbuf_tensor("BIG", [128, 73728], BF16))
        WT = e(nc.sbuf_tensor("WTs", [128, 6144], BF16))
        XR = e(nc.sbuf_tensor("XRs", [128, 2304], F32))
        SQ = e(nc.sbuf_tensor("SQb", [128, 2304], BF16))
        KR = e(nc.sbuf_tensor("KRb", [128, 2304], BF16))
        MISC = e(nc.sbuf_tensor("MISC", [128, 2304], F32))
        A_stk = e(nc.sbuf_tensor("A_stk", [128, 4096], BF16))
        B_stk = e(nc.sbuf_tensor("B_stk", [128, 4096], BF16))
        z_sb = e(nc.sbuf_tensor("z_sb", [128, 2304], BF16))
        BHh = e(nc.sbuf_tensor("BHh", [128, 64], F32))
        GA = e(nc.sbuf_tensor("GA", [128, 64], F32))
        IDB = e(nc.sbuf_tensor("IDB", [128, 128], BF16))
        ID32 = e(nc.sbuf_tensor("ID32", [32, 32], BF16))
        ONEF = e(nc.sbuf_tensor("ONEF", [64, 128], F32))
        ONECB = e(nc.sbuf_tensor("ONECB", [128, 1], BF16))
        SMS = e(nc.sbuf_tensor("SMS", [128, 224], F32))
        PA = e(nc.psum_tensor("PA", [128, 2048], F32))
        PB = e(nc.psum_tensor("PB", [128, 2048], F32))
        sDL = e(nc.semaphore("sDL"))
        sDG = e(nc.semaphore("sDG"))
        sTP = e(nc.semaphore("sTP"))
        sVP = e(nc.semaphore("sVP"))
        sSP = e(nc.semaphore("sSP"))
        block = e(nc.Block())

        DL, DG, TP, VP, SP = Ctr(sDL), Ctr(sDG), Ctr(sTP), Ctr(sVP), Ctr(sSP)


        def vfence(waits=()):
            return P.emit("vector", lambda e_: e_.tensor_copy(SMS[:, 222:223], SMS[:, 222:223]),
                          waits=waits, inc=VP)

        def sfence(waits=()):
            return P.emit("scalar", lambda e_: e_.copy(SMS[:, 223:224], SMS[:, 223:224]),
                          waits=waits, inc=SP)

        def pcols(row0, nrow, s0, sw, pa=None, pb=None):
            pa = PA if pa is None else pa
            pb = PB if pb is None else pb
            if s0 < 2048:
                return pa[row0:row0 + nrow, s0:s0 + sw]
            return pb[row0:row0 + nrow, s0 - 2048:s0 - 2048 + sw]

        t_const = None
        for dst, src in [(GA[:], gateA), (BHh[:, 0:32], biasC), (IDB[:], identb),
                         (ID32[:], id32b), (ONEF[:], onesf), (ONECB[:], onescb)]:
            t_const = P.emit("sync", lambda e_, d=dst, s=src: e_.dma_start(out=d, in_=s),
                             inc=DL, amt=16)
        P.emit("scalar", lambda e_: e_.mul(BHh[:, 32:64], BHh[:, 0:32], 0.5),
               waits=[(DL, t_const)])
        t_bh = sfence()

        sl_An = [(0, 0), (0, 4096)]
        sl_Bn = [(32, 0), (32, 4096)]
        sl_AM, sl_AH, sl_Ash = (0, 8192), (0, 12288), (0, 16384)
        sl_BM, sl_BHa = (32, 8192), (32, 12288)

        def bigf(slot):
            rb, cb = slot
            return BIG[rb:rb + 32, cb:cb + 4096]

        tA_dma = [None] * 16
        tB_dma = [None] * 16
        tA_use = [None] * 16
        tB_use = [None] * 16
        t_ash = P.emit("gpsimd", lambda e_: e_.dma_start(out=bigf(sl_Ash), in_=lashr), inc=DG, amt=16)
        P.emit("gpsimd", lambda e_: e_.dma_start(out=B_stk[32:64, :], in_=lbshrT), inc=DG, amt=16)
        t_bsh = P.emit("gpsimd", lambda e_: e_.dma_start(out=B_stk[64:96, :], in_=lbshrT), inc=DG, amt=16)
        for n in range(16):
            wa = [(VP, tA_use[n - 2])] if n >= 2 else []
            wb = [(VP, tB_use[n - 2])] if n >= 2 else []
            tA_dma[n] = P.emit("gpsimd", lambda e_, d=bigf(sl_An[n % 2]), s=loraA[n]:
                               e_.dma_start(out=d, in_=s), waits=wa, inc=DG, amt=16)
            tB_dma[n] = P.emit("gpsimd", lambda e_, d=bigf(sl_Bn[n % 2]), s=loraBT[n]:
                               e_.dma_start(out=d, in_=s), waits=wb, inc=DG, amt=16)
            wA = [(DG, tA_dma[n]), (DL, t_const)]
            wB = [(DG, tB_dma[n])]
            sa, sb = bigf(sl_An[n % 2]), bigf(sl_Bn[n % 2])
            for (acc, gcol, src, wts) in [(sl_AM, n, sa, wA), (sl_AH, 16 + n, sa, []),
                                          (sl_BM, n, sb, wB), (sl_BHa, 16 + n, sb, [])]:
                g = GA[0:32, gcol:gcol + 1] if acc in (sl_AM, sl_AH) else GA[32:64, gcol:gcol + 1]
                if n == 0:
                    tok = P.emit("vector", lambda e_, a=bigf(acc), s=src, g_=g:
                                 e_.tensor_scalar(a, s, g_, None, op0=ALU.mult),
                                 waits=wts, inc=VP)
                else:
                    tok = P.emit("vector", lambda e_, a=bigf(acc), s=src, g_=g:
                                 e_.scalar_tensor_tensor(a, s, g_, a, op0=ALU.mult, op1=ALU.add),
                                 waits=wts, inc=VP)
                if acc is sl_AH:
                    tA_use[n] = tok
                if acc is sl_BHa:
                    tB_use[n] = tok
        P.emit("gpsimd", lambda e_: e_.dma_start(out=B_stk[0:32, :], in_=bigf(sl_BM)),
               waits=[(VP, tB_use[15]), (DG, t_bsh)], inc=DG, amt=16)
        t_bstk_dg = P.emit("gpsimd", lambda e_: e_.dma_start(out=B_stk[96:128, :], in_=bigf(sl_BHa)),
                           inc=DG, amt=16)
        t_bstk = None

        ev_tok = [None, None]
        t_astk = None
        for d_ in range(DT):
            ping = d_ % 2
            preg = PA[:, 0:48].bitcast(BF16) if ping == 0 else PA[:, 512:560].bitcast(BF16)
            wts = [(VP, tA_use[15]), (DG, t_ash), (DL, t_const)] if d_ == 0 else []
            if ev_tok[ping] is not None:
                wts = wts + [(SP, ev_tok[ping])]
            t_tr = None
            for bi, slot in enumerate([sl_AM, sl_Ash, sl_AH]):
                t_tr = P.emit("tensor", lambda e_, p=preg[:, 32 * bi:32 * bi + 32],
                              s=bigf(slot)[:, 128 * d_:128 * d_ + 128]:
                              e_.transpose(p, s, ID32[:]), waits=wts, inc=TP)
                wts = []
            for k, (pc, ac) in enumerate([(0, 0), (32, 32), (32, 64), (64, 96)]):
                t_astk = P.emit("scalar", lambda e_, p=preg[:, pc:pc + 32],
                                a=A_stk[:, 128 * d_ + ac:128 * d_ + ac + 32]:
                                e_.copy(a, p),
                                waits=[(TP, t_tr)] if k == 0 else [], inc=SP)
            ev_tok[ping] = t_astk
        t_astk = sfence()

        def ln_pass(src, C, big_free, xr_free, psum_free):
            halves = [(0, C // 2), (C // 2, C // 2)] if C == 2048 else [(0, 1152), (1152, 1152)]
            started = set()
            t_sqr = [None, None]
            t_mm = [None, None]
            first_dma = [True, True]
            last_mm = None
            for d_ in range(DT):
                for hi, (h0, hw) in enumerate(halves):
                    wdma = list(xr_free) if first_dma[hi] else [(SP, t_sqr[hi])]
                    first_dma[hi] = False
                    tdma = P.emit("sync", lambda e_, dd=XR[:, h0:h0 + hw],
                                  s=src[128 * d_:128 * d_ + 128, h0:h0 + hw]:
                                  e_.dma_start(out=dd, in_=s), waits=wdma, inc=DL, amt=16)
                    wcast = [(DL, tdma)] + (list(big_free) if (d_ == 0 and hi == 0) else [])
                    t_cast = P.emit("scalar", lambda e_, o=BIG[:, C * d_ + h0:C * d_ + h0 + hw],
                                    s=XR[:, h0:h0 + hw]: e_.copy(o, s), waits=wcast, inc=SP)
                    wsq = [(TP, t_mm[hi])] if t_mm[hi] is not None else []
                    P.emit("scalar", lambda e_, o=SQ[:, h0:h0 + hw],
                           s=XR[:, h0:h0 + hw]:
                           e_.activation(o, s, AF.Square), waits=wsq)
                    t_sqr[hi] = sfence()
                    t_cast = t_sqr[hi]
                    for row, wtok in [(0, t_cast), (32, t_sqr[hi])]:
                        first_piece = True
                        for (s0, sw) in segs(h0 + hw, base=h0):
                            bank = s0 // 512
                            st = (d_ == 0) and (bank not in started)
                            started.add(bank)
                            rhs = BIG[:, C * d_ + s0:C * d_ + s0 + sw] if row == 0 else SQ[:, s0:s0 + sw]
                            wts = [(SP, wtok)] if first_piece else []
                            if d_ == 0 and hi == 0 and row == 0 and first_piece:
                                wts = wts + list(psum_free)
                            first_piece = False
                            last_mm = P.emit("tensor", lambda e_, p=pcols(row, 1, s0, sw), r=rhs, st_=st:
                                             e_.matmul(p, ONECB[:], r, start=st_, stop=False),
                                             waits=wts, inc=TP)
                        t_mm[hi] = last_mm
            inv = 1.0 / D
            fw = [(TP, last_mm)]
            for (s0, sw) in segs(C):
                P.emit("vector", lambda e_, o=MISC[0:1, s0:s0 + sw], p=pcols(0, 1, s0, sw):
                       e_.tensor_scalar(o, p, inv, None, op0=ALU.mult), waits=fw)
                fw = []
            P.emit("vector", lambda e_: e_.tensor_tensor(MISC[32:33, 0:C], MISC[0:1, 0:C],
                                                         MISC[0:1, 0:C], op=ALU.mult))
            for (s0, sw) in segs(C):
                P.emit("vector", lambda e_, o=MISC[64:65, s0:s0 + sw], p=pcols(32, 1, s0, sw),
                       m=MISC[32:33, s0:s0 + sw]:
                       e_.scalar_tensor_tensor(o, p, inv, m, op0=ALU.mult, op1=ALU.subtract))
            t_v = P.emit("vector", lambda e_: e_.tensor_scalar(MISC[64:65, 0:C], MISC[64:65, 0:C],
                                                               EPS, None, op0=ALU.add), inc=VP)
            t_s = P.emit("scalar", lambda e_: e_.activation(MISC[32:33, 0:C], MISC[64:65, 0:C],
                                                            AF.Sqrt), waits=[(VP, t_v)], inc=SP)
            P.emit("vector", lambda e_: e_.reciprocal(MISC[32:33, 0:C], MISC[32:33, 0:C]),
                   waits=[(SP, t_s)])
            t_fin = vfence()
            chunks = [(0, 1024), (1024, 1024)] + ([(2048, 256)] if C == 2304 else [])
            t_ap = {}
            t_bcast = {}
            for ci, (c0, cw) in enumerate(chunks):
                ping = ci % 2
                Pc = PA if ping == 0 else PB
                wts = [(VP, t_fin)]
                if ci >= 2:
                    wts.append((VP, t_ap[ci - 2]))
                t_bc = None
                for roff, mrow in [(0, 0), (1024, 32)]:
                    for (s0, sw) in segs(cw):
                        t_bc = P.emit("tensor", lambda e_, p=Pc[:, roff + s0:roff + s0 + sw],
                                      r=MISC[mrow:mrow + 1, c0 + s0:c0 + s0 + sw],
                                      l=ONEF[mrow:mrow + 1, :]:
                                      e_.matmul(p, l, r, start=True, stop=True),
                                      waits=wts, inc=TP)
                        wts = []
                t_bcast[ci] = t_bc
                for d_ in range(DT):
                    wts = [(TP, t_bc)] if d_ == 0 else []
                    P.emit("vector", lambda e_, t=SQ[:, 0:cw], b=BIG[:, C * d_ + c0:C * d_ + c0 + cw],
                           m=Pc[:, 0:cw]: e_.tensor_tensor(t, b, m, op=ALU.subtract), waits=wts)
                    P.emit("vector", lambda e_, b=BIG[:, C * d_ + c0:C * d_ + c0 + cw],
                           t=SQ[:, 0:cw], r=Pc[:, 1024:1024 + cw]:
                           e_.tensor_tensor(b, t, r, op=ALU.mult))
                t_ap[ci] = vfence()
            t_xln = t_ap[len(chunks) - 1]
            psum_last = [(VP, t_ap[len(chunks) - 2]), (VP, t_xln)]
            xr_last = [(SP, t_sqr[0]), (SP, t_sqr[1])]
            return t_xln, xr_last, psum_last

        def mm_sweep(C, wsl, res_src, out_dst, bstride, lora, halo, scale_bias,
                     xln_ready, psum_free, xr_free, slot_tok, t_z=None):
            halves = [(0, C // 2), (C // 2, C // 2)] if C == 2048 else [(0, 1152), (1152, 1152)]
            dbuf = (C == 2048)
            t_evac = {}
            xr_store = [list(xr_free), list(xr_free)]
            t_slab_last = None
            t_mm_o = {}
            for o in range(DT):
                s0_, s1_ = (2 * o) % 3, (2 * o + 1) % 3
                tw0 = [(TP, slot_tok[s0_])] if slot_tok[s0_] is not None else []
                td0 = P.emit("gpsimd", lambda e_, d=WT[:, 2048 * s0_:2048 * s0_ + 2048],
                             s=wsl[o][:, 0:2048]: e_.dma_start(out=d, in_=s),
                             waits=tw0, inc=DG, amt=16)
                tw1 = [(TP, slot_tok[s1_])] if slot_tok[s1_] is not None else []
                td1 = P.emit("gpsimd", lambda e_, d=WT[:, 2048 * s1_:2048 * s1_ + 2048],
                             s=wsl[o][:, 2048:4096]: e_.dma_start(out=d, in_=s),
                             waits=tw1, inc=DG, amt=16)
                if dbuf:
                    pa = PA if o % 2 == 0 else PB
                    pb = None
                else:
                    pa, pb = PA, PB
                started = set()
                t_mm = None
                for d_ in range(DT):
                    slot = s0_ if d_ < 16 else s1_
                    lhsT = WT[:, 2048 * slot + 128 * (d_ % 16):2048 * slot + 128 * (d_ % 16) + 128]
                    wts = []
                    if d_ == 0:
                        wts += [(DG, td0)] + list(xln_ready)
                        if not dbuf and o > 0:
                            wts.append((VP, t_evac[o - 1]))
                        if dbuf and o > 1:
                            wts.append((VP, t_evac[o - 2]))
                        if o == 0 or (dbuf and o == 1):
                            wts += list(psum_free)
                        if o == 0 and t_z is not None:
                            wts.append((SP, t_z))
                    if d_ == 16:
                        wts.append((DG, td1))
                    for (s0, sw) in segs(C):
                        bank = s0 // 512
                        st = bank not in started
                        started.add(bank)
                        pout = pa[:, s0:s0 + sw] if s0 < 2048 else PB[:, s0 - 2048:s0 - 2048 + sw]
                        t_mm = P.emit("tensor", lambda e_, p=pout, l=lhsT,
                                      r=BIG[:, bstride * d_ + s0:bstride * d_ + s0 + sw], st_=st:
                                      e_.matmul(p, l, r, start=st_, stop=False), waits=wts, inc=TP)
                        wts = []
                    if d_ == 15:
                        slot_tok[s0_] = t_mm
                    if d_ == 31:
                        slot_tok[s1_] = t_mm
                if lora:
                    wts = [(DG, t_bstk_dg), (SP, t_z)] if o == 0 else []
                    own_sg = segs(2048)
                    for i, (s0, sw) in enumerate(own_sg):
                        last = (i == len(own_sg) - 1) and not halo
                        t_mm = P.emit("tensor", lambda e_, p=pa[:, s0:s0 + sw],
                                      l=B_stk[0:64, 128 * o:128 * o + 128],
                                      r=z_sb[0:64, s0:s0 + sw], st_=last:
                                      e_.matmul(p, l, r, start=False, stop=st_), waits=wts, inc=TP)
                        wts = []
                    if halo:
                        t_mm = P.emit("tensor", lambda e_, p=PB[:, 0:256],
                                      l=B_stk[64:128, 128 * o:128 * o + 128],
                                      r=z_sb[64:128, 2048:2304]:
                                      e_.matmul(p, l, r, start=False, stop=True), inc=TP)
                t_mm_o[o] = t_mm
                vw = [(TP, t_mm)]
                if scale_bias:
                    P.emit("vector", lambda e_, p=pa[:, 0:min(C, 2048)], b=BHh[:, 32 + o:33 + o]:
                           e_.tensor_scalar(p, p, 0.5, b, op0=ALU.mult, op1=ALU.add),
                           waits=vw + ([(SP, t_bh)] if o == 0 else []))
                    vw = []
                    if C == 2304:
                        P.emit("vector", lambda e_, b=BHh[:, 32 + o:33 + o]:
                               e_.tensor_scalar(PB[:, 0:256], PB[:, 0:256], 0.5, b,
                                                op0=ALU.mult, op1=ALU.add))
                tok = None
                for hi, (h0, hw) in enumerate(halves):
                    tdx = P.emit("sync", lambda e_, dd=XR[:, h0:h0 + hw],
                                 s=res_src[128 * o:128 * o + 128, h0:h0 + hw]:
                                 e_.dma_start(out=dd, in_=s), waits=xr_store[hi], inc=DL, amt=16)
                    pieces = []
                    c = h0
                    while c < h0 + hw:
                        pw = min((2048 - c) if c < 2048 else (h0 + hw - c), h0 + hw - c)
                        pieces.append((c, pw))
                        c += pw
                    for (c0_, cw_) in pieces:
                        psrc = pa[:, c0_:c0_ + cw_] if c0_ < 2048 else PB[:, c0_ - 2048:c0_ - 2048 + cw_]
                        P.emit("vector", lambda e_, x=XR[:, c0_:c0_ + cw_], p=psrc:
                               e_.tensor_tensor(x, p, x, op=ALU.add),
                               waits=vw + [(DL, tdx)])
                        vw = []
                    tok = vfence()
                    tst = P.emit("gpsimd", lambda e_, dd=out_dst[128 * o:128 * o + 128, h0:h0 + hw],
                                 s=XR[:, h0:h0 + hw]: e_.dma_start(out=dd, in_=s),
                                 waits=[(VP, tok)], inc=DG, amt=16)
                    xr_store[hi] = [(DG, tst)]
                t_evac[o] = tok
            t_last_store = xr_store
            return t_mm_o[DT - 1], [(VP, t_evac[DT - 2]), (VP, t_evac[DT - 1])], t_last_store

        def z_phase(C, bstride, halo, xln_ready, psum_free):
            started = set()
            t_mm = None
            wts = list(xln_ready) + list(psum_free) + [(SP, t_astk)]
            for d_ in range(DT):
                for (s0, sw) in segs(2048 if not halo else 2048):
                    bank = s0 // 512
                    st = (d_ == 0)
                    t_mm = P.emit("tensor", lambda e_, p=PA[0:64, s0:s0 + sw],
                                  l=A_stk[:, 128 * d_:128 * d_ + 64],
                                  r=BIG[:, bstride * d_ + s0:bstride * d_ + s0 + sw], st_=st:
                                  e_.matmul(p, l, r, start=st_, stop=False), waits=wts, inc=TP)
                    wts = []
                if halo:
                    t_mm = P.emit("tensor", lambda e_, p=PB[64:128, 0:256],
                                  l=A_stk[:, 128 * d_ + 64:128 * d_ + 128],
                                  r=BIG[:, bstride * d_ + 2048:bstride * d_ + 2304], st_=(d_ == 0):
                                  e_.matmul(p, l, r, start=st_, stop=False), inc=TP)
            t_z = P.emit("scalar", lambda e_: e_.copy(z_sb[0:64, 0:2048], PA[0:64, 0:2048]),
                         waits=[(TP, t_mm)], inc=SP)
            if halo:
                t_z = P.emit("scalar", lambda e_: e_.copy(z_sb[64:128, 2048:2304], PB[64:128, 0:256]),
                             inc=SP)
            t_z = sfence()
            return t_z

        slot_tok = [None, None, None]

        big_free = [(TP, TP.n), (DG, t_bstk_dg)]
        t_xln1, xr_l1, psum_l1 = ln_pass(xT, C1, big_free, [], [(SP, t_astk)])

        if debug:
            for d_ in range(DT):
                P.emit("gpsimd", lambda e_, o=dbgA[128 * d_:128 * d_ + 128, :],
                       s=BIG[:, C1 * d_:C1 * d_ + C1]: e_.dma_start(out=o, in_=s),
                       waits=[(VP, t_xln1)] if d_ == 0 else [], inc=DG, amt=16)

        t_z1 = z_phase(C1, C1, True, [(VP, t_xln1)], psum_l1)
        t_ffn1_mm, psum_f1, xr_f1 = mm_sweep(
            C1, wf, xT, y1T, C1, lora=True, halo=True, scale_bias=True,
            xln_ready=[(VP, t_xln1)], psum_free=psum_l1, xr_free=xr_l1, slot_tok=slot_tok, t_z=t_z1)

        t_xln2, xr_l2, psum_l2 = ln_pass(y1T, C1,
                                         [(TP, t_ffn1_mm)],
                                         xr_f1[0] + xr_f1[1], psum_f1)

        t_cos = P.emit("sync", lambda e_: e_.dma_start(out=XR[:], in_=cosR),
                       waits=xr_l2, inc=DL, amt=16)
        t_cos = P.emit("sync", lambda e_: e_.dma_start(out=MISC[:], in_=sinR),
                       waits=[(VP, t_xln2)], inc=DL, amt=16)

        t_rope = [None, None]
        t_spill = [None] * NH
        for h in range(NH):
            s0_, s1_ = (2 * h) % 3, (2 * h + 1) % 3
            td0 = P.emit("gpsimd", lambda e_, d=WT[:, 2048 * s0_:2048 * s0_ + 2048],
                         s=wq[h][:, 0:2048]: e_.dma_start(out=d, in_=s),
                         waits=[(TP, slot_tok[s0_])] if slot_tok[s0_] is not None else [],
                         inc=DG, amt=16)
            td1 = P.emit("gpsimd", lambda e_, d=WT[:, 2048 * s1_:2048 * s1_ + 2048],
                         s=wq[h][:, 2048:4096]: e_.dma_start(out=d, in_=s),
                         waits=[(TP, slot_tok[s1_])] if slot_tok[s1_] is not None else [],
                         inc=DG, amt=16)
            ping = h % 2
            Pc = PA if ping == 0 else PB
            t_mm = None
            started = set()
            for d_ in range(DT):
                slot = s0_ if d_ < 16 else s1_
                lhsT = WT[:, 2048 * slot + 128 * (d_ % 16):2048 * slot + 128 * (d_ % 16) + 128]
                wts = []
                if d_ == 0:
                    wts = [(DG, td0), (VP, t_xln2)]
                    if t_rope[ping] is not None:
                        wts.append((VP, t_rope[ping]))
                    elif h <= 1:
                        wts += psum_l2
                if d_ == 16:
                    wts.append((DG, td1))
                for (s0, sw) in segs(2048):
                    t_mm = P.emit("tensor", lambda e_, p=Pc[:, s0:s0 + sw], l=lhsT,
                                  r=BIG[:, C1 * d_ + s0:C1 * d_ + s0 + sw],
                                  st_=(d_ == 0):
                                  e_.matmul(p, l, r, start=st_, stop=False),
                                  waits=wts, inc=TP)
                    wts = []
                if d_ == 15:
                    slot_tok[s0_] = t_mm
                if d_ == 31:
                    slot_tok[s1_] = t_mm
            qbuf = SQ[:, 0:2048] if ping == 0 else KR[:, 0:2048]
            TMPlo = z_sb[0:64, 0:2048]
            TMPhi = z_sb[64:128, 0:2048]
            cosb = XR[:]
            sinb = MISC[:]
            wts = [(TP, t_mm), (DL, t_cos)]
            if h >= 2:
                wts.append((DG, t_spill[h - 2]))
            if h == 0:
                wts.append((TP, t_ffn1_mm))
            P.emit("vector", lambda e_, t=TMPhi, p=Pc[0:64, 0:2048], s=sinb[64:128, 0:2048]:
                   e_.tensor_tensor(t, p, s, op=ALU.mult), waits=wts)
            P.emit("vector", lambda e_, q=qbuf[64:128, 0:2048], p=Pc[64:128, 0:2048],
                   c=cosb[64:128, 0:2048]: e_.tensor_tensor(q, p, c, op=ALU.mult))
            P.emit("vector", lambda e_, q=qbuf[64:128, 0:2048], t=TMPhi:
                   e_.tensor_tensor(q, q, t, op=ALU.add))
            P.emit("vector", lambda e_, t=TMPlo, p=Pc[64:128, 0:2048], s=sinb[0:64, 0:2048]:
                   e_.tensor_tensor(t, p, s, op=ALU.mult))
            P.emit("vector", lambda e_, q=qbuf[0:64, 0:2048], p=Pc[0:64, 0:2048],
                   c=cosb[0:64, 0:2048]: e_.tensor_tensor(q, p, c, op=ALU.mult))
            P.emit("vector", lambda e_, q=qbuf[0:64, 0:2048], t=TMPlo:
                   e_.tensor_tensor(q, q, t, op=ALU.subtract))
            t_rope[ping] = vfence()
            t_spill[h] = P.emit("gpsimd", lambda e_, d=qrS[h], s=qbuf:
                                e_.dma_start(out=d, in_=s),
                                waits=[(VP, t_rope[ping])], inc=DG, amt=16)

        t_kv_mm = [None, None]
        t_krope = None
        t_vvt = None
        for kv in range(2):
            s0_ = (2 * NH + kv) % 3
            td0 = P.emit("gpsimd", lambda e_, d=WT[:, 2048 * s0_:2048 * s0_ + 2048],
                         s=wkv[kv][:, 0:2048]: e_.dma_start(out=d, in_=s),
                         waits=[(TP, slot_tok[s0_])] if slot_tok[s0_] is not None else [],
                         inc=DG, amt=16)
            s1_ = (2 * NH + kv + 1) % 3 if False else s0_
            s1_ = (2 * NH + kv + 2) % 3
            td1 = P.emit("gpsimd", lambda e_, d=WT[:, 2048 * s1_:2048 * s1_ + 2048],
                         s=wkv[kv][:, 2048:4096]: e_.dma_start(out=d, in_=s),
                         waits=[(TP, slot_tok[s1_])] if slot_tok[s1_] is not None else [],
                         inc=DG, amt=16)
            t_mm = None
            for d_ in range(DT):
                slot = s0_ if d_ < 16 else s1_
                lhsT = WT[:, 2048 * slot + 128 * (d_ % 16):2048 * slot + 128 * (d_ % 16) + 128]
                wts = []
                if d_ == 0:
                    wts = [(DG, td0)]
                    if kv == 0:
                        wts += [(VP, t_rope[0]), (VP, t_rope[1])]
                    else:
                        wts += [(VP, t_krope)]
                if d_ == 16:
                    wts.append((DG, td1))
                for (s0, sw) in segs(C1):
                    pout = PA[:, s0:s0 + sw] if s0 < 2048 else PB[:, s0 - 2048:s0 - 2048 + sw]
                    t_mm = P.emit("tensor", lambda e_, p=pout, l=lhsT,
                                  r=BIG[:, C1 * d_ + s0:C1 * d_ + s0 + sw],
                                  st_=(d_ == 0):
                                  e_.matmul(p, l, r, start=st_, stop=False),
                                  waits=wts, inc=TP)
                    wts = []
                if d_ == 15:
                    slot_tok[s0_] = t_mm
                if d_ == 31:
                    slot_tok[s1_] = t_mm
            t_kv_mm[kv] = t_mm
            if kv == 0:
                cosb = XR[:]
                sinb = MISC[:]
                wts = [(TP, t_mm), (DG, t_spill[NH - 1])]
                for (pc, c0_, cw_) in [(PA, 0, 2048), (PB, 2048, 256)]:
                    pb0 = c0_ if pc is PA else 0
                    thi = z_sb[64:128, c0_:c0_ + cw_]
                    tlo = z_sb[0:64, c0_:c0_ + cw_]
                    P.emit("vector", lambda e_, t=thi,
                           p=pc[0:64, pb0:pb0 + cw_], s=sinb[64:128, c0_:c0_ + cw_]:
                           e_.tensor_tensor(t, p, s, op=ALU.mult), waits=wts)
                    wts = []
                    P.emit("vector", lambda e_, q=KR[64:128, c0_:c0_ + cw_],
                           p=pc[64:128, pb0:pb0 + cw_], c=cosb[64:128, c0_:c0_ + cw_]:
                           e_.tensor_tensor(q, p, c, op=ALU.mult))
                    P.emit("vector", lambda e_, q=KR[64:128, c0_:c0_ + cw_], t=thi:
                           e_.tensor_tensor(q, q, t, op=ALU.add))
                    P.emit("vector", lambda e_, t=tlo,
                           p=pc[64:128, pb0:pb0 + cw_], s=sinb[0:64, c0_:c0_ + cw_]:
                           e_.tensor_tensor(t, p, s, op=ALU.mult))
                    P.emit("vector", lambda e_, q=KR[0:64, c0_:c0_ + cw_],
                           p=pc[0:64, pb0:pb0 + cw_], c=cosb[0:64, c0_:c0_ + cw_]:
                           e_.tensor_tensor(q, p, c, op=ALU.mult))
                    P.emit("vector", lambda e_, q=KR[0:64, c0_:c0_ + cw_],
                           t=tlo:
                           e_.tensor_tensor(q, q, t, op=ALU.subtract))
                    t_krope = vfence()
            else:
                wv_ = [(TP, t_mm), (DG, t_spill[NH - 1])]
                P.emit("scalar", lambda e_: e_.copy(SQ[:, 0:2048], PA[:, 0:2048]), waits=wv_)
                P.emit("scalar", lambda e_: e_.copy(SQ[:, 2048:2304], PB[:, 0:256]))
                t_vev = sfence()
                t_tev = [None, None]
                t_vtr = None
                for k in range(18):
                    ping = k % 2
                    preg = PA[:, 0:64].bitcast(BF16) if ping == 0 else PA[:, 512:576].bitcast(BF16)
                    wts = [(SP, t_vev)] if k == 0 else []
                    if t_tev[ping] is not None:
                        wts.append((VP, t_tev[ping]))
                    t_vtr = P.emit("tensor", lambda e_, p=preg, s=SQ[:, 128 * k:128 * k + 128]:
                                   e_.transpose(p, s, IDB[:]), waits=wts, inc=TP)
                    t_tev[ping] = P.emit("vector", lambda e_, p=preg,
                                         d=z_sb[:, 128 * k:128 * k + 128]:
                                         e_.tensor_copy(d, p), waits=[(TP, t_vtr)], inc=VP)
                t_vvt = vfence()

        P.emit("gpsimd", lambda e_: e_.dma_start(out=dbgK, in_=KR[:]),
               waits=[(VP, t_krope)], inc=DG, amt=16)
        t_tabs = P.emit("sync", lambda e_: e_.dma_start(out=XR[:, 0:2048], in_=tabsD),
                        waits=[(VP, t_krope)], inc=DL, amt=16)

        def rt_idx(j, wtile):
            return (2 * j - 2 + wtile) % 18

        t_sig_ring = [None, None]
        t_trsp_sig = [None, None]
        t_trp_ev = [None, None]
        t_av = None
        t_avev = [None, None]
        t_coef = [None] * NH
        t_qs = [None] * 4
        t_evac2 = [None, None]
        TRP = [PB[:, 1024:1536].bitcast(BF16), PB[:, 1536:2048].bitcast(BF16)]

        for j in range(8):
            if j == 0:
                spieces = [(0, 256, 2048), (256, 256, 0), (512, 256, 256)]
            else:
                ws_ = 256 * (j - 1)
                spieces = [(0, 512, ws_), (512, 256, ws_ + 512)]
            for h in range(NH):
                qslot = h % 4
                qs_ap = SQ[:, 256 * qslot:256 * qslot + 256]
                wts = [(DG, t_spill[h])]
                if t_qs[qslot] is not None:
                    wts.append((TP, t_qs[qslot]))
                if j == 0 and h < 4:
                    wts.append((TP, t_vtr))
                tql = P.emit("sync", lambda e_, d=qs_ap, s=qrS[h][:, 256 * j:256 * j + 256]:
                             e_.dma_start(out=d, in_=s), waits=wts, inc=DL, amt=16)
                sring = [PA[:, 0:768], PA[:, 1024:1792]]
                t_sc = [None, None]
                for qt in range(2):
                    ring = (2 * h + qt) % 2
                    wts = [(DL, tql)]
                    if t_sig_ring[ring] is not None:
                        wts.append((SP, t_sig_ring[ring]))
                    elif j == 0 and h == 0:
                        wts += [(VP, t_krope), (VP, t_vvt)]
                    for (oc, ow, kc) in spieces:
                        t_sc[qt] = P.emit("tensor", lambda e_, p=sring[ring][:, oc:oc + ow],
                                          l=qs_ap[:, 128 * qt:128 * qt + 128],
                                          r=KR[:, kc:kc + ow]:
                                          e_.matmul(p, l, r, start=True, stop=True),
                                          waits=wts, inc=TP)
                        wts = []
                t_qs[qslot] = t_sc[1]
                sbuf = WT[:, 1536 * (h % 2):1536 * (h % 2) + 1536]
                t_sg = None
                for qt in range(2):
                    ring = (2 * h + qt) % 2
                    c_ = 2 * h + qt
                    wts = [(TP, t_sc[qt])]
                    if qt == 0:
                        if t_trsp_sig[h % 2] is not None:
                            wts.append((TP, t_trsp_sig[h % 2]))
                        elif j == 0:
                            wts += [(TP, t_) for t_ in [slot_tok[0], slot_tok[1], slot_tok[2]] if t_ is not None]
                        if t_coef[h] is not None:
                            wts.append((VP, t_coef[h]))
                    for r_ in range(3):
                        t_sg = P.emit("scalar", lambda e_, o=sbuf[:, 768 * qt + 256 * r_:768 * qt + 256 * r_ + 256],
                                      p=sring[ring][:, 256 * r_:256 * r_ + 256],
                                      a=SMS[:, 64 * r_ + c_:64 * r_ + c_ + 1]:
                                      e_.activation(o, p, AF.Sigmoid, scale=SCALE, accum_out=a),
                                      waits=wts, inc=SP)
                        wts = []
                t_sg = sfence()
                t_sig_ring[0] = t_sg
                t_sig_ring[1] = t_sg
                r8 = (2 * h) % 8
                SA = [SMS[:, 192 + r8 + q:193 + r8 + q] for q in range(2)]
                SB = [SMS[:, 200 + r8 + q:201 + r8 + q] for q in range(2)]
                RA = [SMS[:, 208 + r8 + q:209 + r8 + q] for q in range(2)]
                RB = [SMS[:, 216 + r8 + q:217 + r8 + q] for q in range(2)]
                tb = lambda t_, cc: XR[:, 256 * j + 64 * t_ + cc:256 * j + 64 * t_ + cc + 1]
                wts = [(SP, t_sg)] + ([(DL, t_tabs)] if (j == 0 and h == 0) else [])
                for qt in range(2):
                    c_ = 2 * h + qt
                    s0c = SMS[:, 0 + c_:1 + c_]
                    s1c = SMS[:, 64 + c_:65 + c_]
                    s2c = SMS[:, 128 + c_:129 + c_]
                    P.emit("vector", lambda e_, o=SA[qt], a_=s0c, b_=s1c:
                           e_.tensor_tensor(o, a_, b_, op=ALU.add), waits=wts)
                    wts = []
                    tk = P.emit("vector", lambda e_, o=SB[qt], a_=s2c, m=tb(2, c_), b_=s1c:
                                e_.scalar_tensor_tensor(o, a_, m, b_, op0=ALU.mult, op1=ALU.add),
                                inc=VP)
                P.emit("vector", None, waits=[(VP, tk)])
                for qt in range(2):
                    P.emit("vector", lambda e_, o=RA[qt], i=SA[qt]: e_.reciprocal(o, i))
                for qt in range(2):
                    tk = P.emit("vector", lambda e_, o=RB[qt], i=SB[qt]: e_.reciprocal(o, i),
                                inc=VP)
                P.emit("vector", None, waits=[(VP, tk)])
                TMP5 = [SQ[:, 1024:1280], SQ[:, 1280:1536]]
                sb_q = lambda r_, qt_: sbuf[:, 768 * qt_ + 256 * r_:768 * qt_ + 256 * r_ + 256]
                for qt in range(2):
                    c_ = 2 * h + qt
                    P.emit("vector", lambda e_, o=sb_q(0, qt), ra=RA[qt], m=tb(1, c_):
                           e_.tensor_scalar(o, o, ra, m, op0=ALU.mult, op1=ALU.mult))
                    P.emit("vector", lambda e_, o=TMP5[qt], i=sb_q(1, qt), ra=RA[qt], m=tb(1, c_):
                           e_.tensor_scalar(o, i, ra, m, op0=ALU.mult, op1=ALU.mult))
                for qt in range(2):
                    c_ = 2 * h + qt
                    P.emit("vector", lambda e_, o=sb_q(1, qt), rb=RB[qt], m=tb(0, c_):
                           e_.tensor_scalar(o, o, rb, m, op0=ALU.mult, op1=ALU.mult))
                    P.emit("vector", lambda e_, o=sb_q(2, qt), rb=RB[qt], m=tb(3, c_):
                           e_.tensor_scalar(o, o, rb, m, op0=ALU.mult, op1=ALU.mult))
                for qt in range(2):
                    tk = P.emit("vector", lambda e_, o=sb_q(1, qt), t=TMP5[qt]:
                                e_.tensor_tensor(o, o, t, op=ALU.add), inc=VP)
                tk = vfence()
                t_coef[h] = tk
                t_scale = tk
                for qt in range(2):
                    t_tt = None
                    for i in range(6):
                        wts = []
                        if i == 0:
                            wts = [(VP, t_scale)]
                            if t_trp_ev[qt] is not None:
                                wts.append((VP, t_trp_ev[qt]))
                        t_tt = P.emit("tensor", lambda e_, p=TRP[qt][:, 128 * i:128 * i + 128],
                                      s=sbuf[:, 768 * qt + 128 * i:768 * qt + 128 * i + 128]:
                                      e_.transpose(p, s, IDB[:]), waits=wts, inc=TP)
                    if qt == 1:
                        t_trsp_sig[h % 2] = t_tt
                    wts = [(TP, t_tt)]
                    if t_av is not None and (h % 2 == 0) and qt == 0:
                        wts.append((TP, t_av))
                    if j == 0 and h <= 1:
                        wts += [(TP, t_) for t_ in [slot_tok[1], slot_tok[2]] if t_ is not None]
                    off = 256 * (h % 2) + 128 * qt
                    t_evac2[qt] = P.emit("vector", lambda e_, o=off, p=TRP[qt]:
                                         e_.tensor_copy(
                                             WT[:, 3072:6144].rearrange("p (i c) -> p i c", c=512)[:, :, o:o + 128],
                                             p[:, 0:768].rearrange("p (i c) -> p i c", c=128)),
                                         waits=wts, inc=VP)
                    t_trp_ev[qt] = t_evac2[qt]
                if h % 2 == 1:
                    t_evac2[1] = vfence()
                    gring = (j * 16 + h // 2) % 2
                    pav = PB[:, 0:512] if gring == 0 else PB[:, 512:1024]
                    wts = [(VP, t_evac2[0]), (VP, t_evac2[1])]
                    if t_avev[gring] is not None:
                        wts.append((SP, t_avev[gring]))
                    if j == 0 and h == 1:
                        wts.append((VP, t_vvt))
                    for i in range(6):
                        rt = rt_idx(j, i)
                        t_av = P.emit("tensor", lambda e_, p=pav,
                                      l=z_sb[:, 128 * rt:128 * rt + 128],
                                      r=WT[:, 3072 + 512 * i:3072 + 512 * i + 512],
                                      st_=(i == 0), sp_=(i == 5):
                                      e_.matmul(p, l, r, start=st_, stop=sp_),
                                      waits=wts, inc=TP)
                        wts = []
                    tv = None
                    for h2 in range(2):
                        hh = h - 1 + h2
                        wts = [(TP, t_av)]
                        if j == 0 and h == 1 and h2 == 0:
                            wts.append((TP, t_kv_mm[1]))
                        tv = P.emit("scalar", lambda e_, d=BIG[:, 2048 * hh + 256 * j:2048 * hh + 256 * j + 256],
                                    p=pav[:, 256 * h2:256 * h2 + 256]:
                                    e_.copy(d, p), waits=wts, inc=SP)
                    t_avev[gring] = sfence()
        t_attn = [(SP, t_avev[0]), (SP, t_avev[1])]
        if debug:
            P.emit("gpsimd", lambda e_: e_.dma_start(out=dbgC, in_=KR[:]),
                   waits=t_attn, inc=DG, amt=16)
            P.emit("gpsimd", lambda e_: e_.dma_start(out=dbgD, in_=SMS[:]), inc=DG, amt=16)
            P.emit("gpsimd", lambda e_: e_.dma_start(out=dbgE, in_=WT[:]), inc=DG, amt=16)
            for d_ in range(DT):
                P.emit("gpsimd", lambda e_, o=dbgB[128 * d_:128 * d_ + 128, :],
                       s=BIG[:, 2048 * d_:2048 * d_ + 2048]: e_.dma_start(out=o, in_=s),
                       inc=DG, amt=16)

        slot_tok = [TP.n, TP.n, TP.n]
        t_wo_mm, psum_wo, xr_wo = mm_sweep(
            C2, wo, y1T[:, 0:2048], y2T, 2048, lora=False, halo=False, scale_bias=False,
            xln_ready=t_attn, psum_free=[(SP, t_avev[0]), (SP, t_avev[1]), (VP, t_trp_ev[0]), (VP, t_trp_ev[1])],
            xr_free=[(VP, t_coef[NH - 1])], slot_tok=slot_tok)

        t_xln3, xr_l3, psum_l3 = ln_pass(y2T, C2, [(TP, t_wo_mm)],
                                         xr_wo[0] + xr_wo[1], psum_wo)

        t_z2 = z_phase(C2, C2, False, [(VP, t_xln3)], psum_l3)
        t_ffn2_mm, psum_f2, xr_f2 = mm_sweep(
            C2, wf, y2T, outT, 2048, lora=True, halo=False, scale_bias=True,
            xln_ready=[(VP, t_xln3)], psum_free=psum_l3, xr_free=xr_l3, slot_tok=slot_tok, t_z=t_z2)

        P.emit("gpsimd", None, waits=[(DG, DG.n)])
        P.emit("sync", None, waits=[(DL, DL.n)])


        @block.sync
        def _(engine):
            P.run("sync", engine)

        @block.gpsimd
        def _(engine):
            P.run("gpsimd", engine)

        @block.tensor
        def _(engine):
            P.run("tensor", engine)

        @block.vector
        def _(engine):
            P.run("vector", engine)

        @block.scalar
        def _(engine):
            P.run("scalar", engine)

    return nc



def _rope_tables():
    inv_freq = 1.0 / 10000.0 ** (np.arange(0, 128, 2, dtype=np.float32) / 128)
    t = np.arange(4096, dtype=np.float32)
    f = np.outer(t, inv_freq)
    cos = np.zeros((4096, 128), dtype=np.float32)
    sin = np.zeros((4096, 128), dtype=np.float32)
    cos[:, 0::2] = np.cos(f)
    cos[:, 1::2] = np.cos(f)
    sin[:, 0::2] = np.sin(f)
    sin[:, 1::2] = np.sin(f)
    return cos, sin


def _slab(w, transpose_in):
    WTm = w.T if transpose_in else w
    a = WTm.reshape(32, 128, 32, 128)
    return np.ascontiguousarray(a.transpose(2, 1, 0, 3))


def _build_inputs(c, x, Wq, Wk, Wv, Wo, gate, kern, bias, lA, lB, lAs, lBs,
                  cos_t, sin_t, shared):
    b, hh = c // 2, c % 2
    if hh == 0:
        own = x[b, 0:2048]
        halo = x[b, 2048:2304]
        pos = np.concatenate([np.arange(0, 2048), np.arange(2048, 2304)])
        gH = gate[c + 1]
    else:
        own = x[b, 2048:4096]
        halo = x[b, 1792:2048]
        pos = np.concatenate([np.arange(2048, 4096), np.arange(1792, 2048)])
        gH = gate[c - 1]
    real = np.concatenate([own, halo], axis=0)
    xT = np.ascontiguousarray(real.T)

    ga = np.zeros((128, 64), dtype=np.float32)
    ga[:, 0:16] = gate[c][None, :]
    ga[:, 16:32] = gH[None, :]
    ga[:, 32] = 1.0

    cosR = np.ascontiguousarray(cos_t[pos].T)
    sinR = np.ascontiguousarray(sin_t[pos].T)

    lin = np.linspace(0.0, 1.0, 256, dtype=np.float32)
    tabs = np.zeros((8, 4, 128, 64), dtype=np.float32)
    for j in range(8):
        A = np.zeros((128, 64), dtype=np.float32)
        for h_ in range(32):
            A[:, 2 * h_] = lin[0:128]
            A[:, 2 * h_ + 1] = lin[128:256]
        OMA = 1.0 - A
        M2 = np.ones((128, 64), dtype=np.float32)
        M3 = np.ones((128, 64), dtype=np.float32)
        if hh == 0 and j == 0:
            A[:] = 1.0
            OMA[:] = 0.0
        if hh == 1 and j == 7:
            M2[:] = 0.0
            M3[:] = 0.0
        tabs[j, 0], tabs[j, 1], tabs[j, 2], tabs[j, 3] = A, OMA, M2, A * M3
    tabsD = np.ascontiguousarray(tabs.transpose(2, 0, 1, 3).reshape(128, 2048))

    d = {
        "xT": xT,
        "biasC": np.ascontiguousarray(bias.reshape(32, 128).T),
        "loraA": lA,
        "loraBT": np.ascontiguousarray(lB.transpose(0, 2, 1)),
        "lashr": lAs,
        "lbshrT": np.ascontiguousarray(lBs.T),
        "gateA": ga,
        "tabsD": tabsD,
        "cosR": cosR,
        "sinR": sinR,
    }
    d.update(shared)
    return d


def kernel(L0_attn_x, L0_attn_Wq, L0_attn_Wk, L0_attn_Wv, L0_attn_Wo,
           L0_ffn_gate, L0_ffn_kernel, L0_ffn_bias, L0_ffn_lora_A, L0_ffn_lora_B,
           L0_ffn_lora_A_shared, L0_ffn_lora_B_shared):
    x = np.asarray(L0_attn_x, dtype=np.float32)
    Wq = np.asarray(L0_attn_Wq, dtype=np.float32)
    Wk = np.asarray(L0_attn_Wk, dtype=np.float32)
    Wv = np.asarray(L0_attn_Wv, dtype=np.float32)
    Wo = np.asarray(L0_attn_Wo, dtype=np.float32)
    gate = np.asarray(L0_ffn_gate, dtype=np.float32)
    kern = np.asarray(L0_ffn_kernel, dtype=np.float32)
    bias = np.asarray(L0_ffn_bias, dtype=np.float32)
    lA = np.asarray(L0_ffn_lora_A, dtype=np.float32)
    lB = np.asarray(L0_ffn_lora_B, dtype=np.float32)
    lAs = np.asarray(L0_ffn_lora_A_shared, dtype=np.float32)
    lBs = np.asarray(L0_ffn_lora_B_shared, dtype=np.float32)

    cos_t, sin_t = _rope_tables()
    wkv = np.stack([
        np.ascontiguousarray(Wk.reshape(128, 32, 128).transpose(2, 1, 0)),
        np.ascontiguousarray(Wv.reshape(128, 32, 128).transpose(2, 1, 0))], axis=0)
    shared = {
        "wq": _slab(Wq, True),
        "wo": _slab(Wo, True),
        "wf": _slab(kern, False),
        "wkv": wkv,
        "id32b": np.eye(32, dtype=np.float32).astype(BF),
        "onesf": np.ones((64, 128), dtype=np.float32),
        "onescb": np.ones((128, 1), dtype=BF),
        "identb": np.eye(128, dtype=np.float32).astype(BF),
    }
    in_maps = [_build_inputs(c, x, Wq, Wk, Wv, Wo, gate, kern, bias,
                             lA, lB, lAs, lBs, cos_t, sin_t, shared)
               for c in range(8)]
    nc = build_program(debug=False)
    res = run_bass_kernel_spmd(nc, in_maps, core_ids=list(range(8)))
    out = np.empty((4, 4096, 4096), dtype=np.float32)
    for c in range(8):
        b, hh = c // 2, c % 2
        out[b, 2048 * hh:2048 * hh + 2048, :] = res.results[c]["outT"].T
    return out
